# version 12
# baseline (speedup 1.0000x reference)
import numpy as np
from contextlib import ExitStack

import concourse.bass as bass
import concourse.mybir as mybir
from concourse.bass_utils import run_bass_kernel_spmd

F32 = mybir.dt.float32
F32R = mybir.dt.float32r
BF16 = mybir.dt.bfloat16
AF = mybir.ActivationFunctionType
ALU = mybir.AluOpType
AXX = mybir.AxisListType.X

NB = 2
T = 2048
D = 1024
H = 16
HD = 64
NCH = 8
NTT = 16
NTC = 4
REG = 5
EPS = 1e-5
SCALE = 0.125

ENGS = ("tensor", "vector", "scalar", "sync", "gpsimd")
DMA_ENGS = ("sync", "gpsimd")


class Emitter:
    def __init__(self, nc, es):
        self.nc = nc
        self.recs = {e: [] for e in ENGS}
        self.sems = {e: es.enter_context(nc.semaphore("sem_" + e)) for e in ENGS}
        self.cc_sem = es.enter_context(nc.semaphore("sem_cc"))
        self.waited = {e: {s: -1 for s in ENGS + ("cc",)} for e in ENGS}
        self.ncc = 0

    def op(self, eng, fn):
        self.recs[eng].append({"k": "op", "fn": fn, "inc": eng in DMA_ENGS})
        return (eng, len(self.recs[eng]) - 1)

    def cc(self, fn):
        self.recs["gpsimd"].append({"k": "cc", "fn": fn})
        self.ncc += 1
        return ("cc", self.ncc)

    def wait(self, eng, token):
        if token is None:
            return
        src, idx = token
        if src == "cc":
            if self.waited[eng]["cc"] >= idx:
                return
            self.waited[eng]["cc"] = idx
            self.recs[eng].append({"k": "waitcc", "n": idx})
            return
        if src == eng and src not in DMA_ENGS:
            return
        if self.waited[eng][src] >= idx:
            return
        self.waited[eng][src] = idx
        self.recs[src][idx]["inc"] = True
        self.recs[eng].append({"k": "wait", "src": src, "idx": idx})

    def barrier(self):
        self.recs["vector"].append({"k": "op", "fn": lambda v: v.drain(),
                                    "inc": False})
        self.recs["scalar"].append({"k": "op", "fn": lambda s: s.drain(),
                                    "inc": False})
        toks = {}
        for e in ENGS:
            for i in range(len(self.recs[e]) - 1, -1, -1):
                r = self.recs[e][i]
                if r["k"] == "op":
                    toks[e] = (e, i)
                    break
                if r["k"] == "cc":
                    toks[e] = ("cc", self.ncc)
                    break
        for e in ENGS:
            for s, t in toks.items():
                if t[0] == "cc" or s != e or s in DMA_ENGS:
                    self.wait(e, t)

    def finalize(self):
        pref = {}
        for e in ENGS:
            c = 0
            lst = []
            step = 16 if e in DMA_ENGS else 1
            for r in self.recs[e]:
                if r["k"] == "op" and r["inc"]:
                    c += step
                lst.append(c)
            pref[e] = lst
        closures = {e: [] for e in ENGS}
        for e in ENGS:
            step = 16 if e in DMA_ENGS else 1
            sem = self.sems[e]
            for r in self.recs[e]:
                if r["k"] == "op":
                    fn = r["fn"]
                    if r["inc"]:
                        closures[e].append(
                            lambda g, fn=fn, sem=sem, step=step: fn(g).then_inc(sem, step))
                    else:
                        closures[e].append(lambda g, fn=fn: fn(g))
                elif r["k"] == "cc":
                    fn = r["fn"]
                    ccs = self.cc_sem
                    closures[e].append(lambda g, fn=fn, ccs=ccs: fn(g).then_inc(ccs, 1))
                elif r["k"] == "wait":
                    val = pref[r["src"]][r["idx"]]
                    wsem = self.sems[r["src"]]
                    closures[e].append(lambda g, wsem=wsem, val=val: g.wait_ge(wsem, val))
                else:
                    ccs = self.cc_sem
                    closures[e].append(lambda g, ccs=ccs, n=r["n"]: g.wait_ge(ccs, n))
        return closures, pref


class Res:
    __slots__ = ("w", "readers")

    def __init__(self):
        self.w = None
        self.readers = []


class KB:
    def __init__(self):
        self.nc = bass.Bass("TRN2", target_bir_lowering=False)
        self.es = ExitStack()
        self.em = Emitter(self.nc, self.es)
        self._declare_io()
        self._alloc_persistent()

    def _do(self, eng, fn, reads, writes):
        em = self.em
        for r in reads:
            em.wait(eng, r.w)
        for w in writes:
            em.wait(eng, w.w)
            for rt in w.readers:
                em.wait(eng, rt)
        tok = em.op(eng, fn)
        for r in reads:
            r.readers.append(tok)
        for w in writes:
            w.w = tok
            w.readers = []
        return tok

    def t(self, fn, reads=(), writes=()):
        return self._do("tensor", fn, reads, writes)

    def v(self, fn, reads=(), writes=()):
        return self._do("vector", fn, reads, writes)

    def s(self, fn, reads=(), writes=()):
        return self._do("scalar", fn, reads, writes)

    def q(self, fn, reads=(), writes=()):
        return self._do("sync", fn, reads, writes)

    def g(self, fn, reads=(), writes=()):
        return self._do("gpsimd", fn, reads, writes)

    def vdrain(self):
        self.em.op("vector", lambda v: v.drain())

    def sdrain(self):
        self.em.op("scalar", lambda s: s.drain())

    def sbuf(self, es, name, shape, dt):
        self._uid = getattr(self, "_uid", 0) + 1
        t = es.enter_context(self.nc.sbuf_tensor("%s_u%d" % (name, self._uid), shape, dt))
        return t, Res()

    def psum(self, es, name, shape, dt=F32):
        self._uid = getattr(self, "_uid", 0) + 1
        t = es.enter_context(self.nc.psum_tensor("%s_u%d" % (name, self._uid), shape, dt))
        return t, Res()

    def _declare_io(self):
        nc = self.nc
        ap = lambda n, s, d, k: nc.dram_tensor(n, s, d, kind=k).ap()
        self.x_in = ap("x_in", [NB, T, D], F32, "ExternalInput")
        self.WqT = ap("WqT", [D, D], F32R, "ExternalInput")
        self.WkT = ap("WkT", [D, D], F32R, "ExternalInput")
        self.WvT = ap("WvT", [D, D], F32R, "ExternalInput")
        self.WoT = ap("WoT", [D, D], F32R, "ExternalInput")
        self.biases = ap("biases", [5, NCH, 128], F32, "ExternalInput")
        self.Wqkv = ap("Wqkv", [D, 3 * D], F32R, "ExternalInput")
        self.WqfT = ap("WqfT", [D, D], F32, "ExternalInput")
        self.Wproj = ap("Wproj", [D, D], F32, "ExternalInput")
        self.qg = ap("qg", [H, HD], F32, "ExternalInput")
        self.ident = ap("ident", [128, 128], F32, "ExternalInput")
        self.cstr = ap("cstr", [128, 2048], F32R, "ExternalInput")
        self.identb = ap("identb", [128, 128], BF16, "ExternalInput")
        self.masks_in = ap("masks_in", [5, 128, 128], F32R, "ExternalInput")
        self.y_out = ap("y_out", [NB, T, D], F32, "ExternalOutput")
        self.xfeat = ap("xfeat", [NB, NCH, 128, T], F32, "Internal")
        self.x2feat = ap("x2feat", [NB, NCH, 128, T], F32, "Internal")
        self.spq = ap("spq", [128, NCH * T], BF16, "Internal")
        self.spk = ap("spk", [128, NCH * T], BF16, "Internal")
        self.spv = ap("spv", [128, NTT * D], BF16, "Internal")
        self.ccb_in = nc.dram_tensor("ccb_in", [1, 16], F32)
        self.ccb_out = nc.dram_tensor("ccb_out", [1, 16], F32)

    def _alloc_persistent(self):
        es = self.es
        self.QF, self.rQF = self.sbuf(es, "QF", [128, NCH * T], BF16)
        self.KF, self.rKF = self.sbuf(es, "KF", [128, NCH * T], BF16)
        self.VT, self.rVT = self.sbuf(es, "VT", [128, NTT * D], BF16)
        self.IDF, self.rIDF = self.sbuf(es, "IDF", [128, 128], F32)
        self.IDB, self.rIDB = self.sbuf(es, "IDB", [128, 128], BF16)
        self.MK, self.rMK = self.sbuf(es, "MK", [128, 5 * 128], F32R)
        self.BIAS, self.rBIAS = self.sbuf(es, "BIAS", [128, 5 * NCH], F32)
        self.ONE1R, self.rONE1R = self.sbuf(es, "ONE1R", [1, 128], F32R)
        self.ONECR, self.rONECR = self.sbuf(es, "ONECR", [128, 1], F32R)
        self.ONECB, self.rONECB = self.sbuf(es, "ONECB", [128, 1], BF16)
        self.BLK64, self.rBLK64 = self.sbuf(es, "BLK64", [128, 2], F32R)
        self.QGN, self.rQGN = self.sbuf(es, "QGN", [16, HD], F32)
        self.QGB, self.rQGB = self.sbuf(es, "QGB", [128, 16], F32)
        self.USB, self.rUSB = self.sbuf(es, "USB", [128, NCH * 16], F32)
        self.SIMT, self.rSIMT = self.sbuf(es, "SIMT", [128, NB * 16], F32)
        self.POS, self.rPOS = self.sbuf(es, "POS", [128, NB * 16], F32)
        self.WPN, self.rWPN = self.sbuf(es, "WPN", [128, 2 * NB * 16], F32)
        self.APART, self.rAPART = self.sbuf(es, "APART", [128, 2 * NB * 16], F32)
        self.RED, self.rRED = self.sbuf(es, "RED", [128, 8], F32)
        self.CCS, self.rCCS = self.sbuf(es, "CCS", [1, 16], F32)
        self.THR, self.rTHR = self.sbuf(es, "THR", [1, 4], F32)
        self.THRR, self.rTHRR = self.sbuf(es, "THRR", [128, 1], F32)
        self.STAT, self.rSTAT = self.sbuf(es, "STAT", [1, 5 * 512], F32)
        self.STR, self.rSTR = self.sbuf(es, "STR", [1, 2 * 512], F32R)
        self.EPSB, self.rEPSB = self.sbuf(es, "EPSB", [1, 1], F32)

    def wslab(self, wdram, wcol0, dout, nr=NCH):
        st = wdram.tensor.shape[1]
        return bass.AP(wdram.tensor, wdram.offset + wcol0 + 128 * dout,
                       [[st, 128], [128 * st, nr], [1, 128]])

    def setup(self, es):
        k = self
        WS, rWS0 = k.sbuf(es, "SWS", [128, 2 * 128], F32)
        rWS = [rWS0, Res()]
        ps = [k.psum(es, "su%d" % i, [128, 512]) for i in range(3)]
        k.q(lambda g: g.dma_start(out=k.IDF[:], in_=k.ident), (), (k.rIDF,))
        k.q(lambda g: g.dma_start(out=k.IDB[:], in_=k.identb), (), (k.rIDB,))
        for i in range(5):
            k.q(lambda g, i=i: g.dma_start(out=k.MK[:, 128 * i:128 * i + 128],
                                           in_=k.masks_in[i]), (), (k.rMK,))
        for i in range(5):
            k.q(lambda g, i=i: g.dma_start(
                out=k.BIAS[:, 8 * i:8 * i + 8],
                in_=k.biases[i].rearrange("a b -> b a"),
                allow_slow_non_contiguous=True), (), (k.rBIAS,))
        k.q(lambda g: g.dma_start(out=k.QGN[:], in_=k.qg), (), (k.rQGN,))
        k.q(lambda g: g.dma_start(out=k.ONECR[:], in_=k.cstr[:, 0:1],
            allow_slow_non_contiguous=True), (), (k.rONECR,))
        k.v(lambda v: v.memset(k.ONECB[:], 1.0), (), (k.rONECB,))
        k.q(lambda g: g.dma_start(out=k.BLK64[:], in_=k.cstr[:, 2:4]), (), (k.rBLK64,))
        k.q(lambda g: g.dma_start(out=k.ONE1R[:], in_=k.cstr[0:1, 1492:1620]),
            (), (k.rONE1R,))
        k.v(lambda v: v.memset(k.EPSB[:], EPS), (), (k.rEPSB,))
        SC, rSC = k.sbuf(es, "SUSC", [16, 4 * HD], F32)
        k.v(lambda v: v.tensor_mul(SC[:, 0:HD], k.QGN[:], k.QGN[:]), (k.rQGN,), (rSC,))
        k.vdrain()
        k.v(lambda v: v.tensor_reduce(k.RED[0:16, 0:1], SC[:, 0:HD], AXX, ALU.add),
            (rSC,), (k.rRED,))
        k.vdrain()
        self.rsqrt_newton(k.RED, k.rRED, 0, 16)
        k.v(lambda v: v.tensor_scalar(k.QGN[:], k.QGN[:], k.RED[0:16, 0:1], 1.0 / H,
                                      ALU.mult, ALU.mult), (k.rRED,), (k.rQGN,))
        k.vdrain()
        p0, rp0 = ps[0]
        k.t(lambda t: t.transpose(p0[0:HD, 0:16], k.QGN[:], k.IDF[0:16, 0:16]),
            (k.rQGN, k.rIDF), (rp0,))
        k.v(lambda v: v.memset(k.QGB[:], 0.0), (), (k.rQGB,))
        k.vdrain()
        k.v(lambda v: v.tensor_copy(k.QGB[0:64, 0:16], p0[0:HD, 0:16]), (rp0,), (k.rQGB,))
        k.vdrain()
        k.g(lambda g: g.dma_start(out=k.QGB[64:128, 1:16:2], in_=k.QGB[0:64, 1:16:2],
                                  allow_slow_non_contiguous=True), (k.rQGB,), (k.rQGB,))
        k.v(lambda v: v.memset(k.QGB[0:64, 1:16:2], 0.0), (), (k.rQGB,))
        k.vdrain()
        for i in range(NCH):
            bank, rbank = ps[1 + (i % 2)]
            for p in range(8):
                sl = p % 2
                k.q(lambda g, p=p, i=i, sl=sl: g.dma_start(
                    out=WS[:, 128 * sl:128 * sl + 128],
                    in_=k.WqfT[128 * p:128 * p + 128, 128 * i:128 * i + 128]),
                    (rWS[sl],), (rWS[sl],))
                k.t(lambda t, p=p, sl=sl, bank=bank: t.matmul(
                    bank[:, 2 * p:2 * p + 2], WS[:, 128 * sl:128 * sl + 128],
                    k.QGB[:, 2 * p:2 * p + 2], start=(p == 0), stop=(p == 7),
                    skip_group_check=True),
                    (rWS[sl], k.rQGB), (rbank,))
            k.v(lambda v, i=i, bank=bank: v.tensor_copy(
                k.USB[:, 16 * i:16 * i + 16], bank[:, 0:16]), (rbank,), (k.rUSB,))

    def rsqrt_newton(self, buf, rbuf, row0, nrows):
        k = self
        a = buf[row0:row0 + nrows, 0:1]
        y = buf[row0:row0 + nrows, 1:2]
        t1 = buf[row0:row0 + nrows, 2:3]
        k.s(lambda s: s.activation(y, a, AF.Ln), (rbuf,), (rbuf,))
        k.sdrain()
        k.s(lambda s: s.activation(y, y, AF.Exp, scale=-0.5), (rbuf,), (rbuf,))
        k.v(lambda v: v.tensor_mul(t1, y, y), (rbuf,), (rbuf,))
        k.vdrain()
        k.v(lambda v: v.scalar_tensor_tensor(t1, t1, 0.5, a, ALU.mult, ALU.mult),
            (rbuf,), (rbuf,))
        k.vdrain()
        k.v(lambda v: v.tensor_scalar(t1, t1, -1.0, 1.5, ALU.mult, ALU.add),
            (rbuf,), (rbuf,))
        k.vdrain()
        k.v(lambda v: v.tensor_mul(a, y, t1), (rbuf,), (rbuf,))
        k.vdrain()

    def stats_pipeline(self, sxb, rsxb, sx2b, rsx2b, mrb, rmrb, irb, rirb):
        k = self
        m = k.STAT[0:1, 0:512]
        var = k.STAT[0:1, 512:1024]
        y0 = k.STAT[0:1, 1024:1536]
        t1 = k.STAT[0:1, 1536:2048]
        k.v(lambda v: v.tensor_scalar_mul(m, sxb[0:1, :], 1.0 / D), (rsxb,), (k.rSTAT,))
        k.v(lambda v: v.tensor_mul(t1, m, m), (k.rSTAT,), (k.rSTAT,))
        k.v(lambda v: v.scalar_tensor_tensor(var, sx2b[0:1, :], 1.0 / D, t1,
                                             ALU.mult, ALU.subtract),
            (rsx2b, k.rSTAT), (k.rSTAT,))
        k.s(lambda s: s.activation(y0, var, AF.Ln, bias=k.EPSB[:]),
            (k.rSTAT, k.rEPSB), (k.rSTAT,))
        k.s(lambda s: s.activation(y0, y0, AF.Exp, scale=-0.5), (k.rSTAT,), (k.rSTAT,))
        k.v(lambda v: v.tensor_mul(t1, y0, y0), (k.rSTAT,), (k.rSTAT,))
        k.v(lambda v: v.scalar_tensor_tensor(t1, t1, 0.5, var, ALU.mult, ALU.mult),
            (k.rSTAT,), (k.rSTAT,))
        k.v(lambda v: v.tensor_scalar(t1, t1, -1.0, 1.5, ALU.mult, ALU.add),
            (k.rSTAT,), (k.rSTAT,))
        k.v(lambda v: v.tensor_mul(y0, y0, t1), (k.rSTAT,), (k.rSTAT,))
        k.v(lambda v: v.tensor_copy(k.STR[0:1, 0:512], m), (k.rSTAT,), (k.rSTR,))
        k.v(lambda v: v.tensor_copy(k.STR[0:1, 512:1024], y0), (k.rSTAT,), (k.rSTR,))
        k.t(lambda t: t.matmul(mrb[:], k.ONE1R[:], k.STR[0:1, 0:512],
                               start=True, stop=True), (k.rSTR, k.rONE1R), (rmrb,))
        k.t(lambda t: t.matmul(irb[:], k.ONE1R[:], k.STR[0:1, 512:1024],
                               start=True, stop=True), (k.rSTR, k.rONE1R), (rirb,))

    def io_in(self, b, es):
        k = self
        XTI, rXTI0 = k.sbuf(es, "XTI", [128, 2 * 1024], F32)
        rXTI = [rXTI0, Res()]
        XST, rXST0 = k.sbuf(es, "XST", [128, 2 * 1024], F32)
        rXST = [rXST0, Res()]
        ps = [k.psum(es, "iop%d" % i, [128, 512]) for i in range(2)]
        for tt in range(NTT):
            sl = tt % 2
            k.q(lambda g, b=b, tt=tt, sl=sl: g.dma_start(
                out=XTI[:, 1024 * sl:1024 * sl + 1024],
                in_=k.x_in[b, 128 * tt:128 * tt + 128, :]), (rXTI[sl],), (rXTI[sl],))
            for dc in range(NCH):
                bank, rbank = ps[dc % 2]
                k.t(lambda t, dc=dc, sl=sl, bank=bank: t.transpose(
                    bank[:, 0:128], XTI[:, 1024 * sl + 128 * dc:1024 * sl + 128 * dc + 128],
                    k.IDF[:]), (rXTI[sl], k.rIDF), (rbank,))
                k.v(lambda v, dc=dc, sl=sl, bank=bank: v.tensor_copy(
                    XST[:, 1024 * sl + 128 * dc:1024 * sl + 128 * dc + 128],
                    bank[:, 0:128]), (rbank,), (rXST[sl],))
            k.g(lambda g, b=b, tt=tt, sl=sl: g.dma_start(
                out=bass.AP(k.xfeat.tensor, b * NCH * 128 * T + 128 * tt,
                            [[T, 128], [128 * T, NCH], [1, 128]]),
                in_=bass.AP(XST, 1024 * sl, [[2048, 128], [128, NCH], [1, 128]])),
                (rXST[sl],), (rXST[sl],))

    def ln_and_projections(self, b, which, es_phase):
        k = self
        src = (k.xfeat if which == "mixer" else k.x2feat)[b]
        if which == "ffn":
            NRMT, rNRMT = k.sbuf(es_phase, "NRMT", [128, 256], F32)
            DOTS, rDOTS = k.sbuf(es_phase, "DOTS", [128, 256], F32)
            SC2, rSC2 = k.sbuf(es_phase, "FSC", [128, 2 * 256], F32)
        for half in range(2):
            tcs = [2 * half, 2 * half + 1]
            with ExitStack() as es:
                LNRF, rLNRF = k.sbuf(es, "LNRF", [128, NCH * 1024], F32R)
                XC, rXC0 = k.sbuf(es, "XC", [128, 2 * 512], F32)
                rXC = [rXC0, Res()]
                XR, rXR0 = k.sbuf(es, "XR", [128, 2 * 1024], F32R)
                rXR = [rXR0, Res()]
                TDF, rTDF0 = k.sbuf(es, "TDF", [128, 2 * 512], F32)
                rTDF = [rTDF0, Res()]
                if which == "ffn":
                    LNF, rLNF0 = k.sbuf(es, "LNF", [128, 2 * 512], F32)
                    rLNF = [rLNF0, Res()]
                with ExitStack() as esp:
                    sxb, rsxb = k.psum(esp, "lsx", [1, 512])
                    sx2b, rsx2b = k.psum(esp, "lsx2", [1, 512])
                    mrb, rmrb = k.psum(esp, "lmr", [128, 512])
                    irb, rirb = k.psum(esp, "lir", [128, 512])
                    if which == "ffn":
                        dp, rdp = k.psum(esp, "ldp", [128, 64])
                    for tl, tc in enumerate(tcs):
                        for dc in range(NCH):
                            sl = dc % 2
                            k.q(lambda g, tc=tc, dc=dc, sl=sl: g.dma_start(
                                out=XC[:, 512 * sl:512 * sl + 512],
                                in_=bass.AP(src.tensor,
                                            src.offset + dc * 128 * T + 512 * tc,
                                            [[T, 128], [1, 512]])),
                                (rXC[sl],), (rXC[sl],))
                            k.v(lambda v, sl=sl: v.tensor_copy(
                                XR[:, 1024 * sl:1024 * sl + 512],
                                XC[:, 512 * sl:512 * sl + 512]), (rXC[sl],), (rXR[sl],))
                            k.v(lambda v, sl=sl: v.tensor_mul(
                                XR[:, 1024 * sl + 512:1024 * sl + 1024],
                                XC[:, 512 * sl:512 * sl + 512],
                                XC[:, 512 * sl:512 * sl + 512]), (rXC[sl],), (rXR[sl],))
                            k.t(lambda t, dc=dc, sl=sl: t.matmul(
                                sxb[0:1, :], k.ONECR[:], XR[:, 1024 * sl:1024 * sl + 512],
                                start=(dc == 0), stop=(dc == NCH - 1)),
                                (rXR[sl], k.rONECR), (rsxb,))
                            k.t(lambda t, dc=dc, sl=sl: t.matmul(
                                sx2b[0:1, :], k.ONECR[:],
                                XR[:, 1024 * sl + 512:1024 * sl + 1024],
                                start=(dc == 0), stop=(dc == NCH - 1)),
                                (rXR[sl], k.rONECR), (rsx2b,))
                        self.stats_pipeline(sxb, rsxb, sx2b, rsx2b, mrb, rmrb, irb, rirb)
                        for dc in range(NCH):
                            sl = dc % 2
                            k.g(lambda g, tc=tc, dc=dc, sl=sl: g.dma_start(
                                out=XC[:, 512 * sl:512 * sl + 512],
                                in_=bass.AP(src.tensor,
                                            src.offset + dc * 128 * T + 512 * tc,
                                            [[T, 128], [1, 512]])),
                                (rXC[sl],), (rXC[sl],))
                            k.v(lambda v, sl=sl: v.tensor_sub(
                                TDF[:, 512 * sl:512 * sl + 512],
                                XC[:, 512 * sl:512 * sl + 512], mrb[:]),
                                (rXC[sl], rmrb), (rTDF[sl],))
                            dstc = 1024 * dc + 512 * tl
                            if which == "mixer":
                                k.v(lambda v, sl=sl, dstc=dstc: v.tensor_mul(
                                    LNRF[:, dstc:dstc + 512],
                                    TDF[:, 512 * sl:512 * sl + 512], irb[:]),
                                    (rTDF[sl], rirb), (rLNRF,))
                            else:
                                k.v(lambda v, sl=sl: v.tensor_mul(
                                    LNF[:, 512 * sl:512 * sl + 512],
                                    TDF[:, 512 * sl:512 * sl + 512], irb[:]),
                                    (rTDF[sl], rirb), (rLNF[sl],))
                                k.v(lambda v, sl=sl, dstc=dstc: v.tensor_copy(
                                    LNRF[:, dstc:dstc + 512],
                                    LNF[:, 512 * sl:512 * sl + 512]),
                                    (rLNF[sl],), (rLNRF,))
                                for j in range(4):
                                    k.t(lambda t, dc=dc, j=j, sl=sl, tl=tl: t.matmul(
                                        dp[:, 16 * j:16 * j + 16],
                                        LNF[:, 512 * sl + 128 * j:512 * sl + 128 * j + 128],
                                        k.USB[:, 16 * dc:16 * dc + 16],
                                        start=(dc == 0 and j == 0),
                                        stop=(dc == NCH - 1 and j == 3),
                                        skip_group_check=True),
                                        (rLNF[sl], k.rUSB), (rdp,))
                        if which == "ffn":
                            k.v(lambda v, tc=tc: v.tensor_copy(
                                DOTS[:, 64 * tc:64 * tc + 64], dp[:, 0:64]),
                                (rdp,), (rDOTS,))
                with ExitStack() as esp:
                    ps8 = [k.psum(esp, "pj%d" % i, [128, 512]) for i in range(4)]
                    pb = [k.psum(esp, "pjb%d" % i, [128, 128], BF16) for i in range(2)]
                    WC, rWC0 = k.sbuf(esp, "PWC", [128, 2 * 1024], F32R)
                    rWC = [rWC0, Res()]
                    VS, rVS0 = k.sbuf(esp, "PVS", [128, 2 * 512], BF16)
                    rVS = [rVS0, Res()]
                    if which == "ffn":
                        QSQ, rQSQ = k.sbuf(esp, "QSQ", [128, 512], F32R)
                        npb, rnpb = k.psum(esp, "pnp", [128, 64])

                    def gen(wget, out_cb):
                        for dout in range(NCH):
                            wsl = dout % 2
                            k.q(lambda g, dout=dout, wsl=wsl: g.dma_start(
                                out=WC[:, 1024 * wsl:1024 * wsl + 1024],
                                in_=wget(dout)), (rWC[wsl],), (rWC[wsl],))
                            for tl in range(2):
                                bank, rbank = ps8[2 * (dout % 2) + tl]
                                for r in range(NCH):
                                    k.t(lambda t, r=r, tl=tl, wsl=wsl, bank=bank: t.matmul(
                                        bank[:],
                                        WC[:, 1024 * wsl + 128 * r:1024 * wsl + 128 * r + 128],
                                        LNRF[:, 1024 * r + 512 * tl:1024 * r + 512 * tl + 512],
                                        start=(r == 0), stop=(r == NCH - 1)),
                                        (rWC[wsl], rLNRF), (rbank,))
                                out_cb(dout, tcs[tl], bank, rbank)

                    def qk_out(dst, rdst, bcol):
                        def cb(dout, tc, bank, rbank):
                            if bcol is not None:
                                k.s(lambda s, dout=dout, tc=tc, bank=bank: s.activation(
                                    dst[:, T * dout + 512 * tc:T * dout + 512 * tc + 512],
                                    bank[:], AF.Identity,
                                    bias=k.BIAS[:, bcol * 8 + dout:bcol * 8 + dout + 1]),
                                    (rbank, k.rBIAS), (rdst,))
                            else:
                                k.s(lambda s, dout=dout, tc=tc, bank=bank: s.activation(
                                    dst[:, T * dout + 512 * tc:T * dout + 512 * tc + 512],
                                    bank[:], AF.Copy), (rbank,), (rdst,))
                        return cb

                    def q_out_norms(dout, tc, bank, rbank):
                        k.s(lambda s, dout=dout, tc=tc, bank=bank: s.activation(
                            k.QF[:, T * dout + 512 * tc:T * dout + 512 * tc + 512],
                            bank[:], AF.Copy), (rbank,), (k.rQF,))
                        k.s(lambda s, bank=bank: s.activation(QSQ[:], bank[:], AF.Square),
                            (rbank,), (rQSQ,))
                        for j in range(4):
                            k.t(lambda t, j=j: t.matmul(
                                npb[:, 16 * j:16 * j + 2], QSQ[:, 128 * j:128 * j + 128],
                                k.BLK64[:], start=(j == 0), stop=(j == 3),
                                skip_group_check=True), (rQSQ, k.rBLK64), (rnpb,))
                        k.v(lambda v, tc=tc, dout=dout: v.tensor_copy(
                            bass.AP(NRMT, 16 * (4 * tc) + 2 * dout,
                                    [[256, 128], [16, 4], [1, 2]]),
                            bass.AP(npb[:].tensor, 0, [[64, 128], [16, 4], [1, 2]])),
                            (rnpb,), (rNRMT,))

                    def v_out(bcol):
                        def cb(dout, tc, bank, rbank):
                            sl = tc % 2
                            if bcol is not None:
                                k.s(lambda s, dout=dout, bank=bank, sl=sl: s.activation(
                                    VS[:, 512 * sl:512 * sl + 512], bank[:], AF.Identity,
                                    bias=k.BIAS[:, bcol * 8 + dout:bcol * 8 + dout + 1]),
                                    (rbank, k.rBIAS), (rVS[sl],))
                            else:
                                k.s(lambda s, bank=bank, sl=sl: s.activation(
                                    VS[:, 512 * sl:512 * sl + 512], bank[:], AF.Copy),
                                    (rbank,), (rVS[sl],))
                            for j in range(4):
                                pbk, rpbk = pb[j % 2]
                                k.t(lambda t, j=j, sl=sl, pbk=pbk: t.transpose(
                                    pbk[:, 0:128],
                                    VS[:, 512 * sl + 128 * j:512 * sl + 128 * j + 128],
                                    k.IDB[:]), (rVS[sl], k.rIDB), (rpbk,))
                                tt = 4 * tc + j
                                k.v(lambda v, tt=tt, dout=dout, pbk=pbk: v.tensor_copy(
                                    k.VT[:, D * tt + 128 * dout:D * tt + 128 * dout + 128],
                                    pbk[:, 0:128]), (rpbk,), (k.rVT,))
                        return cb

                    if which == "mixer":
                        gen(lambda d: self.wslab(k.WqT, 0, d), qk_out(k.QF, k.rQF, 0))
                        gen(lambda d: self.wslab(k.WkT, 0, d), qk_out(k.KF, k.rKF, 1))
                        gen(lambda d: self.wslab(k.WvT, 0, d), v_out(2))
                    else:
                        gen(lambda d: self.wslab(k.Wqkv, 0, d), q_out_norms)
                        gen(lambda d: self.wslab(k.Wqkv, 1024, d), qk_out(k.KF, k.rKF, None))
                        gen(lambda d: self.wslab(k.Wqkv, 2048, d), v_out(None))
                self.em.barrier()
        if which == "ffn" and getattr(self, "dbg_ffn_hook", None) is not None:
            self.dbg_ffn_hook(b, NRMT, rNRMT, DOTS, rDOTS)
        if which == "ffn":
            k.v(lambda v: v.tensor_scalar(NRMT[:], NRMT[:], 1e-30, None, ALU.max),
                (rNRMT,), (rNRMT,))
            k.s(lambda s: s.activation(SC2[:, 0:256], NRMT[:], AF.Ln), (rNRMT,), (rSC2,))
            k.s(lambda s: s.activation(SC2[:, 0:256], SC2[:, 0:256], AF.Exp, scale=-0.5),
                (rSC2,), (rSC2,))
            k.v(lambda v: v.tensor_mul(SC2[:, 256:512], SC2[:, 0:256], SC2[:, 0:256]),
                (rSC2,), (rSC2,))
            k.v(lambda v: v.scalar_tensor_tensor(SC2[:, 256:512], SC2[:, 256:512], 0.5,
                                                 NRMT[:], ALU.mult, ALU.mult),
                (rNRMT, rSC2), (rSC2,))
            k.v(lambda v: v.tensor_scalar(SC2[:, 256:512], SC2[:, 256:512], -1.0, 1.5,
                                          ALU.mult, ALU.add), (rSC2,), (rSC2,))
            k.v(lambda v: v.tensor_mul(SC2[:, 0:256], SC2[:, 0:256], SC2[:, 256:512]),
                (rSC2,), (rSC2,))
            k.v(lambda v: v.tensor_mul(SC2[:, 0:256], SC2[:, 0:256], DOTS[:]),
                (rDOTS,), (rSC2,))
            k.v(lambda v, b=b: v.tensor_reduce(
                k.SIMT[:, 16 * b:16 * b + 16],
                bass.AP(SC2, 0, [[512, 128], [16, 16], [1, 16]]), AXX, ALU.add),
                (rSC2,), (k.rSIMT,))

    def mixer_attn(self, b, es):
        k = self
        PTG, rPTG = k.sbuf(es, "PTG", [128, 512], F32R)
        VPE, rVPE = k.sbuf(es, "VPE", [128, 7 * 72], F32R)
        VPO, rVPO = k.sbuf(es, "VPO", [128, 7 * 136], F32R)
        k.q(lambda g: g.dma_start(out=VPE[:], in_=k.cstr[:, 36:540]), (), (rVPE,))
        k.q(lambda g: g.dma_start(out=VPO[:], in_=k.cstr[:, 540:1492]), (), (rVPO,))
        PTW, rPTW0 = k.sbuf(es, "PTW", [128, 5 * 512], F32R)
        rPTW = [rPTW0, Res(), Res(), Res(), Res()]
        PTQ0, rPTQ0 = k.sbuf(es, "PTQ0", [128, 32], F32R)
        k.q(lambda g: g.dma_start(out=PTQ0[:], in_=k.cstr[:, 4:36]), (), (rPTQ0,))
        REPS, rREPS = k.sbuf(es, "REPS", [128, 512], F32R)
        ARM, rARM = k.sbuf(es, "ARM", [128, NCH * 512], F32R)
        YC, rYC = k.sbuf(es, "YC", [128, NCH * 512], F32)
        YR, rYR0 = k.sbuf(es, "YR", [128, 2 * 1024], F32R)
        rYR = [rYR0, Res()]
        XC2, rXC20 = k.sbuf(es, "XC2", [128, 2 * 512], F32)
        rXC2 = [rXC20, Res()]
        TDF2, rTDF20 = k.sbuf(es, "TDF2", [128, 2 * 512], F32)
        rTDF2 = [rTDF20, Res()]
        WC, rWC0 = k.sbuf(es, "OWC", [128, 2 * 1024], F32R)
        rWC = [rWC0, Res()]
        gb, rgb = k.psum(es, "agb", [128, 512])
        wb0, rwb0 = k.psum(es, "awb0", [128, 512])
        wb1, rwb1 = k.psum(es, "awb1", [128, 512])
        q0b, rq0b = k.psum(es, "aq0", [128, 32])
        ab, rab = k.psum(es, "aab", [128, 512])
        rb, rrb = k.psum(es, "arb", [128, 512])
        ib2, rib2 = k.psum(es, "aib", [128, 512])
        wbs = [(wb0, rwb0), (wb1, rwb1)]
        MD = k.MK[:, 0:128]
        MD0 = k.MK[:, 128:256]
        MP = k.MK[:, 256:384]
        MP0 = k.MK[:, 384:512]
        MR0 = k.MK[:, 512:640]
        for qc in range(NTC):
            for h in range(H):
                dc, off = h // 2, 64 * (h % 2)
                even = (h % 2 == 0)

                def kf(j):
                    return k.KF[off:off + 64, T * dc + 128 * j:T * dc + 128 * j + 128]

                def qf(c0, n):
                    return k.QF[off:off + 64, T * dc + c0:T * dc + c0 + n]

                if qc == 0:
                    for j in range(1, 16):
                        k.t(lambda t, j=j, a=kf(j), bqq=qf(0, 2): t.matmul(
                            q0b[:, 2 * j:2 * j + 2], a, bqq,
                            start=(j == 1), stop=(j == 15), skip_group_check=True),
                            (k.rKF, k.rQF), (rq0b,))
                    k.s(lambda s: s.activation(
                        bass.AP(PTQ0, 2, [[32, 128], [2, 15]]),
                        bass.AP(q0b[:].tensor, 2, [[32, 128], [2, 15]]),
                        AF.Exp, scale=SCALE), (rq0b,), (rPTQ0,))
                k.t(lambda t, a=kf(0), bqq=qf(512 * qc, 512): t.matmul(
                    gb[:], a, bqq, start=True, stop=True), (k.rKF, k.rQF), (rgb,))
                k.s(lambda s: s.activation(PTG[:], gb[:], AF.Exp, scale=SCALE),
                    (rgb,), (rPTG,))
                for i4 in range(4):
                    i = 4 * qc + i4
                    if i == 0:
                        k.v(lambda v, i4=i4: v.tensor_mul(
                            PTG[:, 128 * i4:128 * i4 + 128],
                            PTG[:, 128 * i4:128 * i4 + 128], MD0), (k.rMK,), (rPTG,))
                    elif i == 1:
                        k.v(lambda v, i4=i4: v.tensor_mul(
                            PTG[:, 128 * i4:128 * i4 + 128],
                            PTG[:, 128 * i4:128 * i4 + 128], MP0), (k.rMK,), (rPTG,))
                    else:
                        k.v(lambda v, i4=i4: v.tensor_mul(
                            PTG[:, 128 * i4:128 * i4 + 128],
                            PTG[:, 128 * i4:128 * i4 + 128], MR0), (k.rMK,), (rPTG,))
                jlist = [j for j in range(4 * qc - 1, 4 * qc + 4) if 1 <= j <= 15]
                jslot = {}
                for idx, j in enumerate(jlist):
                    sl = idx
                    jslot[j] = sl
                    wbk, rwbk = wbs[idx % 2]
                    q0c = max(128 * j, 512 * qc)
                    q1c = min(128 * j + 256, 512 * qc + 512, T)
                    n = q1c - q0c
                    k.t(lambda t, n=n, wbk=wbk, a=kf(j), bqq=qf(q0c, n): t.matmul(
                        wbk[:, 0:n], a, bqq, start=True, stop=True),
                        (k.rKF, k.rQF), (rwbk,))
                    k.s(lambda s, n=n, sl=sl, wbk=wbk: s.activation(
                        PTW[:, 512 * sl:512 * sl + n], wbk[:, 0:n], AF.Exp, scale=SCALE),
                        (rwbk,), (rPTW[sl],))
                    for pc in range((n + 127) // 128):
                        qt = (q0c + 128 * pc) // 128
                        mask = MD if qt == j else MP
                        k.v(lambda v, pc=pc, sl=sl, mask=mask: v.tensor_mul(
                            PTW[:, 512 * sl + 128 * pc:512 * sl + 128 * pc + 128],
                            PTW[:, 512 * sl + 128 * pc:512 * sl + 128 * pc + 128], mask),
                            (k.rMK,), (rPTW[sl],))
                VP, rVP, stride, vcol = ((VPE, rVPE, 72, 0) if even
                                         else (VPO, rVPO, 136, 64))

                def build_vp(sl, j):
                    k.v(lambda v,
                        dstap=VP[:, stride * sl + vcol:stride * sl + vcol + 64],
                        srcap=k.VT[:, D * j + HD * h:D * j + HD * h + 64]:
                        v.tensor_copy(dstap, srcap), (k.rVT,), (rVP,))

                def lhs_vp(sl):
                    return (VP[:, stride * sl:stride * sl + 65] if even
                            else VP[:, stride * sl:stride * sl + 128])

                build_vp(0, 0)
                for idx, j in enumerate(jlist):
                    build_vp(1 + idx, j)
                plans = []
                for i4 in range(4):
                    i = 4 * qc + i4
                    plans.append(("vp", 0, rPTG, PTG[:, 128 * i4:128 * i4 + 128],
                                  128 * i4))
                    for j in (i - 1, i):
                        if j in jslot:
                            sl = jslot[j]
                            q0c = max(128 * j, 512 * qc)
                            rel = 128 * i - q0c
                            plans.append(("vp", 1 + sl, rPTW[sl],
                                          PTW[:, 512 * sl + rel:512 * sl + rel + 128],
                                          128 * i4))
                    if i == 0:
                        for j in range(1, 16):
                            if j in jslot:
                                plans.append(("vp", 1 + jslot[j], rPTQ0,
                                              PTQ0[:, 2 * j:2 * j + 2], 0))
                            else:
                                plans.append(("rebuild", j, rPTQ0,
                                              PTQ0[:, 2 * j:2 * j + 2], 0))
                for pi, (kind, arg, rres, rap, col0) in enumerate(plans):
                    if kind == "rebuild":
                        build_vp(6, arg)
                        sl = 6
                    else:
                        sl = arg
                    w = rap.shape[-1]
                    mm = 65 if even else 128
                    k.t(lambda t, lhs=lhs_vp(sl), rap=rap, col0=col0, w=w, mm=mm,
                        st=(pi == 0), sp=(pi == len(plans) - 1): t.matmul(
                        ab[0:mm, col0:col0 + w], lhs, rap, start=st, stop=sp,
                        skip_group_check=True), (rVP, rres), (rab,))
                drow = 64 if even else 32
                nrow0 = 0 if even else 64
                k.v(lambda v, drow=drow: v.reciprocal(
                    k.STAT[0:1, 2048:2560], ab[drow:drow + 1, :]), (rab,), (k.rSTAT,))
                k.v(lambda v: v.tensor_copy(k.STR[0:1, 0:512], k.STAT[0:1, 2048:2560]),
                    (k.rSTAT,), (k.rSTR,))
                k.t(lambda t: t.matmul(rb[:], k.ONE1R[:], k.STR[0:1, 0:512],
                                       start=True, stop=True), (k.rSTR, k.rONE1R), (rrb,))
                k.v(lambda v: v.tensor_copy(REPS[:], rb[:]), (rrb,), (rREPS,))
                k.v(lambda v, dc=dc, nrow0=nrow0: v.tensor_mul(
                    ARM[nrow0:nrow0 + 64, 512 * dc:512 * dc + 512],
                    ab[nrow0:nrow0 + 64, :], REPS[nrow0:nrow0 + 64, :]),
                    (rab, rREPS), (rARM,))
            if getattr(self, "dbg_arm_hook", None) is not None:
                self.dbg_arm_hook(b, qc, ARM, rARM)
            ybs = [(gb, rgb), (wb0, rwb0)]
            sxb, rsxb = wb1, rwb1
            sx2b, rsx2b = rb, rrb
            mrb, rmrb = ab, rab
            irb, rirb = ib2, rib2
            for dout in range(NCH):
                ybk, rybk = ybs[dout % 2]
                wsl = dout % 2
                k.q(lambda g, dout=dout, wsl=wsl: g.dma_start(
                    out=WC[:, 1024 * wsl:1024 * wsl + 1024],
                    in_=self.wslab(k.WoT, 0, dout)), (rWC[wsl],), (rWC[wsl],))
                for r in range(NCH):
                    k.t(lambda t, r=r, wsl=wsl, ybk=ybk: t.matmul(
                        ybk[:], WC[:, 1024 * wsl + 128 * r:1024 * wsl + 128 * r + 128],
                        ARM[:, 512 * r:512 * r + 512], start=(r == 0),
                        stop=(r == NCH - 1)), (rWC[wsl], rARM), (rybk,))
                k.s(lambda s, dout=dout, ybk=ybk: s.activation(
                    YC[:, 512 * dout:512 * dout + 512], ybk[:], AF.Identity,
                    bias=k.BIAS[:, 3 * 8 + dout:3 * 8 + dout + 1]),
                    (rybk, k.rBIAS), (rYC,))
                sl = dout % 2
                k.v(lambda v, dout=dout, sl=sl: v.tensor_copy(
                    YR[:, 1024 * sl:1024 * sl + 512], YC[:, 512 * dout:512 * dout + 512]),
                    (rYC,), (rYR[sl],))
                k.v(lambda v, dout=dout, sl=sl: v.tensor_mul(
                    YR[:, 1024 * sl + 512:1024 * sl + 1024],
                    YC[:, 512 * dout:512 * dout + 512],
                    YC[:, 512 * dout:512 * dout + 512]), (rYC,), (rYR[sl],))
                k.t(lambda t, dout=dout, sl=sl: t.matmul(
                    sxb[0:1, :], k.ONECR[:], YR[:, 1024 * sl:1024 * sl + 512],
                    start=(dout == 0), stop=(dout == NCH - 1)),
                    (rYR[sl], k.rONECR), (rsxb,))
                k.t(lambda t, dout=dout, sl=sl: t.matmul(
                    sx2b[0:1, :], k.ONECR[:], YR[:, 1024 * sl + 512:1024 * sl + 1024],
                    start=(dout == 0), stop=(dout == NCH - 1)),
                    (rYR[sl], k.rONECR), (rsx2b,))
            self.stats_pipeline(sxb, rsxb, sx2b, rsx2b, mrb, rmrb, irb, rirb)
            for dc2 in range(NCH):
                sl = dc2 % 2
                k.q(lambda g, b=b, qc=qc, dc2=dc2, sl=sl: g.dma_start(
                    out=XC2[:, 512 * sl:512 * sl + 512],
                    in_=bass.AP(k.xfeat.tensor,
                                b * NCH * 128 * T + dc2 * 128 * T + 512 * qc,
                                [[T, 128], [1, 512]])), (rXC2[sl],), (rXC2[sl],))
                k.v(lambda v, dc2=dc2, sl=sl: v.tensor_sub(
                    TDF2[:, 512 * sl:512 * sl + 512], YC[:, 512 * dc2:512 * dc2 + 512],
                    mrb[:]), (rYC, rmrb), (rTDF2[sl],))
                k.v(lambda v, sl=sl: v.tensor_mul(
                    TDF2[:, 512 * sl:512 * sl + 512], TDF2[:, 512 * sl:512 * sl + 512],
                    irb[:]), (rirb,), (rTDF2[sl],))
                k.v(lambda v, sl=sl: v.tensor_add(
                    TDF2[:, 512 * sl:512 * sl + 512], TDF2[:, 512 * sl:512 * sl + 512],
                    XC2[:, 512 * sl:512 * sl + 512]), (rXC2[sl],), (rTDF2[sl],))
                k.g(lambda g, b=b, qc=qc, dc2=dc2, sl=sl: g.dma_start(
                    out=bass.AP(k.x2feat.tensor,
                                b * NCH * 128 * T + dc2 * 128 * T + 512 * qc,
                                [[T, 128], [1, 512]]),
                    in_=TDF2[:, 512 * sl:512 * sl + 512]), (rTDF2[sl],), (rTDF2[sl],))

    def collective_masks(self, es):
        k = self
        if getattr(self, "dbg_simt_early", None) is not None:
            self.dbg_simt_early()
        SIMS, rSIMS = k.sbuf(es, "SIMS", [128, 2 * NB * 16], F32)
        bank, rbank = k.psum(es, "ccp", [128, 512])
        k.v(lambda v: v.tensor_copy(SIMS[:, 0:32], k.SIMT[:, 0:32]),
            (k.rSIMT,), (rSIMS,))
        k.vdrain()
        k.v(lambda v: v.tensor_copy(SIMS[:, 32:64], k.SIMT[:, 0:32]),
            (k.rSIMT,), (rSIMS,))
        k.vdrain()
        for b in range(NB):
            k.v(lambda v, b=b: v.memset(SIMS[0:REG, 16 * b:16 * b + 1], -1e30),
                (), (rSIMS,))
            k.v(lambda v, b=b: v.memset(SIMS[0:REG, 32 + 16 * b:32 + 16 * b + 1], 1e30),
                (), (rSIMS,))
        k.vdrain()
        if getattr(self, "dbg_sims_hook", None) is not None:
            self.dbg_sims_hook(SIMS, rSIMS)
        k.v(lambda v: v.tensor_reduce(k.RED[:, 0:1], SIMS[:, 0:32], AXX, ALU.max),
            (rSIMS,), (k.rRED,))
        k.vdrain()
        k.v(lambda v: v.tensor_reduce(k.RED[:, 1:2], SIMS[:, 32:64], AXX, ALU.min),
            (rSIMS,), (k.rRED,))
        k.vdrain()
        k.v(lambda v: v.tensor_scalar_mul(k.RED[:, 1:2], k.RED[:, 1:2], -1.0),
            (k.rRED,), (k.rRED,))
        k.vdrain()
        k.t(lambda t: t.transpose(bank[0:2, 0:128], k.RED[:, 0:2], k.IDF[:]),
            (k.rRED, k.rIDF), (rbank,))
        k.v(lambda v: v.tensor_reduce(k.RED[0:2, 4:5], bank[0:2, 0:128], AXX, ALU.max),
            (rbank,), (k.rRED,))
        k.vdrain()
        k.v(lambda v: v.memset(k.CCS[:], -1e30), (), (k.rCCS,))
        k.vdrain()
        k.g(lambda g: g.dma_start(out=k.ccb_in.ap(), in_=k.CCS[:]), (k.rCCS,), (k.rCCS,))
        k.g(lambda g: g.dma_start(out=k.ccb_in.ap()[0:1, 0:2], in_=k.RED[0:2, 4:5],
                                  allow_slow_non_contiguous=True),
            (k.rRED,), (k.rCCS,))
        k.em.wait("gpsimd", k.rCCS.w)
        cctok = k.em.cc(lambda g: g.collective_compute(
            "AllReduce", ALU.max, replica_groups=[list(range(8))],
            ins=[k.ccb_in.ap().opt()], outs=[k.ccb_out.ap().opt()]))
        k.em.wait("gpsimd", cctok)
        k.g(lambda g: g.dma_start(out=k.CCS[:], in_=k.ccb_out.ap()), (k.rCCS,), (k.rCCS,))
        k.v(lambda v: v.tensor_scalar_mul(k.THR[0:1, 0:1], k.CCS[0:1, 0:1], 0.9),
            (k.rCCS,), (k.rTHR,))
        k.vdrain()
        k.v(lambda v: v.scalar_tensor_tensor(k.THR[0:1, 0:1], k.CCS[0:1, 1:2], -0.1,
                                             k.THR[0:1, 0:1], ALU.mult, ALU.add),
            (k.rCCS,), (k.rTHR,))
        k.vdrain()
        k.v(lambda v: v.tensor_copy(k.STR[0:1, 0:1], k.THR[0:1, 0:1]),
            (k.rTHR,), (k.rSTR,))
        k.vdrain()
        k.v(lambda v: v.memset(k.THR[0:1, 2:3], 0.0), (), (k.rTHR,))
        k.vdrain()
        k.v(lambda v: v.tensor_copy(k.STR[0:1, 1:2], k.THR[0:1, 2:3]),
            (k.rTHR,), (k.rSTR,))
        k.vdrain()
        k.t(lambda t: t.matmul(bank[:, 0:2], k.ONE1R[:], k.STR[0:1, 0:2],
                               start=True, stop=True), (k.rSTR, k.rONE1R), (rbank,))
        k.v(lambda v: v.tensor_copy(k.THRR[:], bank[:, 0:1]), (rbank,), (k.rTHRR,))
        k.vdrain()
        k.v(lambda v: v.tensor_scalar(k.POS[:], k.SIMT[:], k.THRR[:, 0:1], None,
                                      ALU.is_gt), (k.rSIMT, k.rTHRR), (k.rPOS,))
        k.vdrain()
        for b in range(NB):
            k.v(lambda v, b=b: v.tensor_copy(
                bass.AP(k.WPN, 2 * 16 * b, [[2 * NB * 16, 128], [2, 16], [1, 1]]),
                k.POS[:, 16 * b:16 * b + 16]), (k.rPOS,), (k.rWPN,))
            k.vdrain()
            k.v(lambda v, b=b: v.tensor_scalar(
                bass.AP(k.WPN, 2 * 16 * b + 1, [[2 * NB * 16, 128], [2, 16], [1, 1]]),
                k.POS[:, 16 * b:16 * b + 16], -1.0, 1.0, ALU.mult, ALU.add),
                (k.rPOS,), (k.rWPN,))
            k.vdrain()
            k.v(lambda v, b=b: v.memset(k.WPN[0:REG, 2 * 16 * b:2 * 16 * b + 2], 1.0),
                (), (k.rWPN,))
            k.vdrain()
            k.v(lambda v, b=b: v.tensor_copy(
                bass.AP(k.APART, 2 * 16 * b, [[2 * NB * 16, 128], [2, 16], [1, 1]]),
                k.POS[:, 16 * b:16 * b + 16]), (k.rPOS,), (k.rAPART,))
            k.vdrain()
            k.v(lambda v, b=b: v.tensor_scalar(
                bass.AP(k.APART, 2 * 16 * b + 1, [[2 * NB * 16, 128], [2, 16], [1, 1]]),
                k.POS[:, 16 * b:16 * b + 16], -1.0, 1.0, ALU.mult, ALU.add),
                (k.rPOS,), (k.rAPART,))
            k.vdrain()
            k.v(lambda v, b=b: v.memset(k.APART[0:REG, 2 * 16 * b:2 * 16 * b + 2], 0.5),
                (), (k.rAPART,))
        k.vdrain()

    def ffn_attn(self, b, es_out, FOUT, rFOUT):
        k = self
        with ExitStack() as es:
            VMF, rVMF = k.sbuf(es, "VMF", [128, NTT * 1040], BF16)
            PT, rPT0 = k.sbuf(es, "FPT", [128, 2 * 1024], BF16)
            rPT = [rPT0, Res()]
            RF, rRF = k.sbuf(es, "RF", [128, 16], F32)
            TMPQ, rTMPQ = k.sbuf(es, "TMPQ", [128, 256], F32)
            sp = [k.psum(es, "fsp%d" % i, [128, 1024]) for i in range(2)]
            avp = [k.psum(es, "fav%d" % i, [128, 130]) for i in range(4)]
            for half in range(2):
              h0 = 8 * half
              for j in range(NTT):
                for w in range(2):
                    k.v(lambda v, j=j, w=w, b=b, h0=h0: v.tensor_scalar(
                        bass.AP(VMF, 1040 * j + 64 * w, [[NTT * 1040, 128], [130, 8], [1, 64]]),
                        bass.AP(k.VT, D * j + HD * h0, [[NTT * D, 128], [64, 8], [1, 64]]),
                        k.WPN[:, 2 * (16 * b + j) + w:2 * (16 * b + j) + w + 1], None,
                        ALU.mult), (k.rVT, k.rWPN), (rVMF,))
                k.v(lambda v, j=j, b=b: v.tensor_copy(
                    bass.AP(VMF, 1040 * j + 128, [[NTT * 1040, 128], [130, 8], [1, 2]]),
                    bass.AP(k.WPN, 2 * (16 * b + j), [[2 * NB * 16, 128], [0, 8], [1, 2]])),
                    (k.rWPN,), (rVMF,))
              for hh in range(8):
                h = h0 + hh
                dc, off = h // 2, 64 * (h % 2)
                for qc in range(NTC):
                    avfirst = [True] * 4
                    for jp in range(NTT // 2):
                        spk, rspk = sp[jp % 2]
                        for jj in range(2):
                            j = 2 * jp + jj
                            k.t(lambda t, jj=jj, spk=spk,
                                a=k.KF[off:off + 64, T * dc + 128 * j:T * dc + 128 * j + 128],
                                bqq=k.QF[off:off + 64, T * dc + 512 * qc:T * dc + 512 * qc + 512]:
                                t.matmul(spk[:, 512 * jj:512 * jj + 512], a, bqq,
                                         start=True, stop=True, skip_group_check=True),
                                (k.rKF, k.rQF), (rspk,))
                        k.s(lambda s, jp=jp, spk=spk: s.activation(
                            PT[:, 1024 * (jp % 2):1024 * (jp % 2) + 1024], spk[:],
                            AF.Exp, scale=SCALE), (rspk,), (rPT[jp % 2],))
                        for jj in range(2):
                            j = 2 * jp + jj
                            for i4 in range(4):
                                abk, rabk = avp[i4]
                                st = avfirst[i4]
                                avfirst[i4] = False
                                ptc = 1024 * (jp % 2) + 512 * jj + 128 * i4
                                k.t(lambda t, ptc=ptc, j=j, hh=hh, st=st, abk=abk: t.matmul(
                                    abk[:], PT[:, ptc:ptc + 128],
                                    VMF[:, 1040 * j + 130 * hh:1040 * j + 130 * hh + 130],
                                    start=st, stop=(j == NTT - 1),
                                    skip_group_check=True),
                                    (rPT[jp % 2], rVMF), (rabk,))
                    for i4 in range(4):
                        abk, rabk = avp[i4]
                        k.v(lambda v, abk=abk, i4=i4: v.reciprocal(
                            RF[:, 4 * i4:4 * i4 + 2], abk[:, 128:130]),
                            (rabk,), (rRF,))
                    for i4 in range(4):
                        tt = 4 * qc + i4
                        k.v(lambda v, b=b, tt=tt, i4=i4: v.tensor_mul(
                            RF[:, 4 * i4 + 2:4 * i4 + 4], RF[:, 4 * i4:4 * i4 + 2],
                            k.APART[:, 2 * (16 * b + tt):2 * (16 * b + tt) + 2]),
                            (k.rAPART,), (rRF,))
                    for i4 in range(4):
                        abk, rabk = avp[i4]
                        k.v(lambda v, abk=abk, i4=i4: v.tensor_scalar(
                            TMPQ[:, 64 * i4:64 * i4 + 64], abk[:, 0:64],
                            RF[:, 4 * i4 + 2:4 * i4 + 3], None, ALU.mult),
                            (rabk, rRF), (rTMPQ,))
                    for i4 in range(4):
                        abk, rabk = avp[i4]
                        tt = 4 * qc + i4
                        k.v(lambda v, abk=abk, tt=tt, h=h, i4=i4: v.scalar_tensor_tensor(
                            FOUT[:, D * tt + HD * h:D * tt + HD * h + 64],
                            abk[:, 64:128], RF[:, 4 * i4 + 3:4 * i4 + 4],
                            TMPQ[:, 64 * i4:64 * i4 + 64], ALU.mult, ALU.add),
                            (rabk, rRF, rTMPQ), (rFOUT,))
            self.em.barrier()

    def ffn_proj_out(self, b, es, FOUT, rFOUT):
        k = self
        FFNF, rFFNF = k.sbuf(es, "FFNF", [128, NCH * 512], BF16)
        YC, rYC = k.sbuf(es, "FYC", [128, NCH * 512], BF16)
        YR, rYR0 = k.sbuf(es, "FYR", [128, 2 * 1024], BF16)
        rYR = [rYR0, Res()]
        XC2, rXC20 = k.sbuf(es, "FXC", [128, 2 * 512], F32)
        rXC2 = [rXC20, Res()]
        TDF2, rTDF20 = k.sbuf(es, "FTD", [128, 2 * 512], F32)
        rTDF2 = [rTDF20, Res()]
        OST, rOST0 = k.sbuf(es, "OST", [128, 4 * 1024], F32)
        rOST = [rOST0, Res(), Res(), Res()]
        WS, rWS0 = k.sbuf(es, "FWS", [128, 2 * 1024], F32)
        rWS = [rWS0, Res()]
        WC, rWC0 = k.sbuf(es, "FWC", [128, 2 * 1024], BF16)
        rWC = [rWC0, Res()]
        pb, rpb = k.psum(es, "opb", [128, 128], BF16)
        yb = [k.psum(es, "oyb%d" % i, [128, 512]) for i in range(2)]
        sxb, rsxb = k.psum(es, "osx", [1, 512])
        sx2b, rsx2b = k.psum(es, "osx2", [1, 512])
        mrb, rmrb = k.psum(es, "omr", [128, 512])
        irb, rirb = k.psum(es, "oir", [128, 512])
        pout, rpout = k.psum(es, "opo", [128, 128])
        for qc in range(NTC):
            for dc in range(NCH):
                for i4 in range(4):
                    tt = 4 * qc + i4
                    k.t(lambda t, tt=tt, dc=dc: t.transpose(
                        pb[:, 0:128], FOUT[:, D * tt + 128 * dc:D * tt + 128 * dc + 128],
                        k.IDB[:]), (rFOUT, k.rIDB), (rpb,))
                    k.v(lambda v, dc=dc, i4=i4: v.tensor_copy(
                        FFNF[:, 512 * dc + 128 * i4:512 * dc + 128 * i4 + 128],
                        pb[:, 0:128]), (rpb,), (rFFNF,))
            for dout in range(NCH):
                ybk, rybk = yb[dout % 2]
                wsl = dout % 2
                k.q(lambda g, dout=dout, wsl=wsl: g.dma_start(
                    out=WS[:, 1024 * wsl:1024 * wsl + 1024],
                    in_=self.wslab(k.Wproj, 0, dout)), (rWS[wsl],), (rWS[wsl],))
                k.v(lambda v, wsl=wsl: v.tensor_copy(
                    WC[:, 1024 * wsl:1024 * wsl + 1024],
                    WS[:, 1024 * wsl:1024 * wsl + 1024]), (rWS[wsl],), (rWC[wsl],))
                for r in range(NCH):
                    k.t(lambda t, r=r, wsl=wsl, ybk=ybk: t.matmul(
                        ybk[:], WC[:, 1024 * wsl + 128 * r:1024 * wsl + 128 * r + 128],
                        FFNF[:, 512 * r:512 * r + 512], start=(r == 0),
                        stop=(r == NCH - 1)), (rWC[wsl], rFFNF), (rybk,))
                k.s(lambda s, dout=dout, ybk=ybk: s.activation(
                    YC[:, 512 * dout:512 * dout + 512], ybk[:], AF.Identity,
                    bias=k.BIAS[:, 4 * 8 + dout:4 * 8 + dout + 1]),
                    (rybk, k.rBIAS), (rYC,))
                sl = dout % 2
                k.v(lambda v, dout=dout, sl=sl: v.tensor_copy(
                    YR[:, 1024 * sl:1024 * sl + 512], YC[:, 512 * dout:512 * dout + 512]),
                    (rYC,), (rYR[sl],))
                k.v(lambda v, dout=dout, sl=sl: v.tensor_mul(
                    YR[:, 1024 * sl + 512:1024 * sl + 1024],
                    YC[:, 512 * dout:512 * dout + 512],
                    YC[:, 512 * dout:512 * dout + 512]), (rYC,), (rYR[sl],))
                k.t(lambda t, dout=dout, sl=sl: t.matmul(
                    sxb[0:1, :], k.ONECB[:], YR[:, 1024 * sl:1024 * sl + 512],
                    start=(dout == 0), stop=(dout == NCH - 1)),
                    (rYR[sl], k.rONECB), (rsxb,))
                k.t(lambda t, dout=dout, sl=sl: t.matmul(
                    sx2b[0:1, :], k.ONECB[:], YR[:, 1024 * sl + 512:1024 * sl + 1024],
                    start=(dout == 0), stop=(dout == NCH - 1)),
                    (rYR[sl], k.rONECB), (rsx2b,))
            self.stats_pipeline(sxb, rsxb, sx2b, rsx2b, mrb, rmrb, irb, rirb)
            for dc in range(NCH):
                sl = dc % 2
                k.q(lambda g, b=b, qc=qc, dc=dc, sl=sl: g.dma_start(
                    out=XC2[:, 512 * sl:512 * sl + 512],
                    in_=bass.AP(k.x2feat.tensor,
                                b * NCH * 128 * T + dc * 128 * T + 512 * qc,
                                [[T, 128], [1, 512]])), (rXC2[sl],), (rXC2[sl],))
                k.v(lambda v, dc=dc, sl=sl: v.tensor_sub(
                    TDF2[:, 512 * sl:512 * sl + 512], YC[:, 512 * dc:512 * dc + 512],
                    mrb[:]), (rYC, rmrb), (rTDF2[sl],))
                k.v(lambda v, sl=sl: v.tensor_mul(
                    TDF2[:, 512 * sl:512 * sl + 512], TDF2[:, 512 * sl:512 * sl + 512],
                    irb[:]), (rirb,), (rTDF2[sl],))
                k.v(lambda v, sl=sl: v.tensor_add(
                    TDF2[:, 512 * sl:512 * sl + 512], TDF2[:, 512 * sl:512 * sl + 512],
                    XC2[:, 512 * sl:512 * sl + 512]), (rXC2[sl],), (rTDF2[sl],))
                for i4 in range(4):
                    k.t(lambda t, sl=sl, i4=i4: t.transpose(
                        pout[:, 0:128],
                        TDF2[:, 512 * sl + 128 * i4:512 * sl + 128 * i4 + 128],
                        k.IDF[:]), (rTDF2[sl], k.rIDF), (rpout,))
                    osl = i4
                    k.v(lambda v, dc=dc, osl=osl: v.tensor_copy(
                        OST[:, 1024 * osl + 128 * dc:1024 * osl + 128 * dc + 128],
                        pout[:, 0:128]), (rpout,), (rOST[osl],))
            for i4 in range(4):
                tt = 4 * qc + i4
                osl = i4
                k.g(lambda g, b=b, tt=tt, osl=osl: g.dma_start(
                    out=k.y_out[b, 128 * tt:128 * tt + 128, :],
                    in_=OST[:, 1024 * osl:1024 * osl + 1024]), (rOST[osl],), (rOST[osl],))

    def spill(self):
        k = self
        k.g(lambda g: g.dma_start(out=k.spq, in_=k.QF[:]), (k.rQF,), (k.rQF,))
        k.g(lambda g: g.dma_start(out=k.spk, in_=k.KF[:]), (k.rKF,), (k.rKF,))
        k.g(lambda g: g.dma_start(out=k.spv, in_=k.VT[:]), (k.rVT,), (k.rVT,))

    def reload(self):
        k = self
        k.q(lambda g: g.dma_start(out=k.QF[:], in_=k.spq), (k.rQF,), (k.rQF,))
        k.q(lambda g: g.dma_start(out=k.KF[:], in_=k.spk), (k.rKF,), (k.rKF,))
        k.q(lambda g: g.dma_start(out=k.VT[:], in_=k.spv), (k.rVT,), (k.rVT,))

    def build(self):
        k = self
        with ExitStack() as es1:
            self.setup(es1)
            k.em.barrier()
        for b in range(NB):
            with ExitStack() as es1:
                self.io_in(b, es1)
                k.em.barrier()
            with ExitStack() as es1:
                self.ln_and_projections(b, "mixer", es1)
                k.em.barrier()
            with ExitStack() as es1:
                self.mixer_attn(b, es1)
                k.em.barrier()
            with ExitStack() as es1:
                self.ln_and_projections(b, "ffn", es1)
                k.em.barrier()
            if b == 0:
                self.spill()
                k.em.barrier()
        with ExitStack() as es1:
            self.collective_masks(es1)
            k.em.barrier()
        if getattr(self, "dbg_post_masks", None) is not None:
            self.dbg_post_masks()
            k.em.barrier()
        for b in (1, 0):
            if b == 0:
                self.reload()
                k.em.barrier()
            with ExitStack() as es1:
                FOUT, rFOUT = k.sbuf(es1, "FOUT", [128, NTT * D], BF16)
                self.ffn_attn(b, es1, FOUT, rFOUT)
                if getattr(self, "dbg_fout_hook", None) is not None:
                    self.dbg_fout_hook(b, FOUT, rFOUT)
                self.ffn_proj_out(b, es1, FOUT, rFOUT)
                k.em.barrier()
        closures, pref = k.em.finalize()
        gp_total = pref["gpsimd"][-1] if pref["gpsimd"] else 0
        sem_g = k.em.sems["gpsimd"]
        closures["gpsimd"].append(lambda g, sem_g=sem_g, v=gp_total: g.wait_ge(sem_g, v))
        with k.nc.Block() as block:
            @block.tensor
            def _(eng):
                for fn in closures["tensor"]:
                    fn(eng)

            @block.vector
            def _(eng):
                for fn in closures["vector"]:
                    fn(eng)

            @block.scalar
            def _(eng):
                for fn in closures["scalar"]:
                    fn(eng)

            @block.sync
            def _(eng):
                for fn in closures["sync"]:
                    fn(eng)

            @block.gpsimd
            def _(eng):
                for fn in closures["gpsimd"]:
                    fn(eng)
        k.es.close()
        return k.nc


def _host_inputs(inputs):
    f = np.float32
    x = np.ascontiguousarray(np.asarray(inputs["L0_attn_x"], dtype=f))
    WqT = np.ascontiguousarray(np.asarray(inputs["L0_attn_Wq"], dtype=f).T)
    WkT = np.ascontiguousarray(np.asarray(inputs["L0_attn_Wk"], dtype=f).T)
    WvT = np.ascontiguousarray(np.asarray(inputs["L0_attn_Wv"], dtype=f).T)
    WoT = np.ascontiguousarray(np.asarray(inputs["L0_attn_Wo"], dtype=f).T)
    biases = np.ascontiguousarray(np.stack([
        np.asarray(inputs["L0_attn_bq"], dtype=f),
        np.asarray(inputs["L0_attn_bk"], dtype=f),
        np.asarray(inputs["L0_attn_bv"], dtype=f),
        np.asarray(inputs["L0_attn_bo"], dtype=f),
        np.asarray(inputs["L0_ffn_b_proj"], dtype=f)]).reshape(5, NCH, 128))
    Wqkv = np.ascontiguousarray(np.asarray(inputs["L0_ffn_W_qkv"], dtype=f))
    WqfT = np.ascontiguousarray(Wqkv[:, :D].T)
    Wproj = np.ascontiguousarray(np.asarray(inputs["L0_ffn_W_proj"], dtype=f))
    qgall = np.asarray(inputs["L0_ffn_g_info"], dtype=f)
    qg = np.ascontiguousarray(qgall[0, 0].reshape(H, HD))
    ident = np.eye(128, dtype=f)
    import ml_dtypes
    identb = np.eye(128, dtype=ml_dtypes.bfloat16)
    kk = np.arange(128)[:, None]
    qq = np.arange(128)[None, :]
    mdiag = (kk <= qq).astype(f)
    mdiag0 = ((kk <= qq) | (qq == 0)).astype(f)
    mprev = (kk >= qq).astype(f)
    mprev0 = ((kk >= qq) | (kk == 0)).astype(f)
    mrow0 = (kk == 0).astype(f) * np.ones((1, 128), f)
    masks = np.ascontiguousarray(np.stack([mdiag, mdiag0, mprev, mprev0, mrow0]))
    cstr = np.zeros((128, 2048), f)
    cstr[:, 0] = 1.0
    cstr[0:64, 2] = 1.0
    cstr[64:128, 3] = 1.0
    for s5 in range(7):
        cstr[:, 36 + 72 * s5 + 64] = 1.0
        cstr[:, 540 + 136 * s5 + 32] = 1.0
    cstr[0, 1492:1620] = 1.0
    base = {"WqT": WqT, "WkT": WkT, "WvT": WvT, "WoT": WoT, "biases": biases,
            "Wqkv": Wqkv, "WqfT": WqfT, "Wproj": Wproj, "qg": qg,
            "ident": ident, "identb": identb, "masks_in": masks, "cstr": cstr}
    in_maps = []
    for c in range(8):
        m = dict(base)
        m["x_in"] = np.ascontiguousarray(x[NB * c:NB * c + NB])
        in_maps.append(m)
    return in_maps


_NC_CACHE = {}


def kernel(**inputs):
    if "nc" not in _NC_CACHE:
        _NC_CACHE["nc"] = KB().build()
    nc = _NC_CACHE["nc"]
    in_maps = _host_inputs(inputs)
    res = run_bass_kernel_spmd(nc, in_maps, core_ids=list(range(8)))
    out = np.concatenate([np.asarray(r["y_out"]) for r in res.results], axis=0)
    return np.ascontiguousarray(out.astype(np.float32))
